# version 01
# baseline (speedup 1.0000x reference)
import numpy as np
import concourse.bass as bass
import concourse.mybir as mybir
from concourse.bass_utils import run_bass_kernel_spmd

F32 = mybir.dt.float32
BF16 = mybir.dt.bfloat16
I8 = mybir.dt.int8
AF = mybir.ActivationFunctionType
ALU = mybir.AluOpType
AX = mybir.AxisListType

P = 128
S = 2048
H = 1024
II = 2816
NT = S // P
HC = H // P
ICN = II // P
QMAX = 127.0
EPS_LN = 1e-5
EPS_NRM = 1e-9

ENGS = ("sync", "gpsimd", "scalar", "vector", "tensor")
DMA_QUEUES = ("sync", "gpsimd", "scalar")

RUN_FFN1 = True
RUN_ATTN = True
RUN_FFN3 = True


class Op:
    __slots__ = ("eng", "fn", "deps", "dma", "inc", "needs_inc", "count_after",
                 "gidx", "tag")

    def __init__(self, eng, fn, deps, dma, gidx, tag, inc=0):
        self.eng = eng
        self.fn = fn
        self.deps = deps
        self.dma = dma
        self.inc = inc
        self.needs_inc = False
        self.count_after = 0
        self.gidx = gidx
        self.tag = tag


class KB:

    def __init__(self):
        self.per_eng = {e: [] for e in ENGS}
        self.writer = {}
        self.readers = {}
        self.n = 0

    def op(self, eng, fn, r=(), w=(), dma=False, coll=False, tag=""):
        deps = set()
        rr, ww = list(r), list(w)
        rr2 = []
        for res in rr:
            if res.startswith("PB"):
                ww.append(res)
            else:
                rr2.append(res)
        for res in rr2:
            wop = self.writer.get(res)
            if wop is not None:
                deps.add(wop)
        for res in ww:
            wop = self.writer.get(res)
            if wop is not None:
                deps.add(wop)
            for rop in self.readers.get(res, {}).values():
                deps.add(rop)
        o = Op(eng, fn, deps, dma, self.n, tag)
        o.deps.discard(o)
        self.n += 1
        for res in rr2:
            self.readers.setdefault(res, {})[eng] = o
        for res in ww:
            self.writer[res] = o
            self.readers[res] = {}
        self.per_eng[eng].append(o)
        if coll:
            o.inc = 1
            o.needs_inc = True
        return o

    def finalize(self, nc):
        for eng in ENGS:
            for o in self.per_eng[eng]:
                for d in o.deps:
                    if d.eng == "tensor" and o.eng == "tensor":
                        continue
                    d.needs_inc = True
        counts = {}
        for eng in ENGS:
            c = 0
            for o in self.per_eng[eng]:
                if o.dma:
                    o.inc = 16
                elif o.needs_inc and o.inc == 0:
                    o.inc = 1
                c += o.inc
                o.count_after = c
            counts[eng] = c
        sems = {e: nc.alloc_semaphore(name="sem_" + e) for e in ENGS}
        waited = {}

        def body_for(engname):
            def body(proxy):
                for o in self.per_eng[engname]:
                    byeng = {}
                    for d in o.deps:
                        if d.eng == "tensor" and engname == "tensor":
                            continue
                        assert d.gidx < o.gidx, (d.tag, o.tag)
                        assert d.inc > 0, (d.tag, o.tag)
                        if d.count_after > byeng.get(d.eng, 0):
                            byeng[d.eng] = d.count_after
                    for deng in sorted(byeng):
                        v = byeng[deng]
                        key = (engname, deng)
                        if waited.get(key, 0) >= v:
                            continue
                        proxy.wait_ge(sems[deng], v)
                        waited[key] = v
                    inst = o.fn(proxy)
                    if o.inc:
                        inst.then_inc(sems[o.eng], o.inc)
                if engname in DMA_QUEUES and counts[engname] > 0:
                    proxy.wait_ge(sems[engname], counts[engname])
            return body

        with nc.Block() as block:
            block.sync(body_for("sync"))
            block.gpsimd(body_for("gpsimd"))
            block.scalar(body_for("scalar"))
            block.vector(body_for("vector"))
            block.tensor(body_for("tensor"))
        return counts


def _regs(prefix, start_el, n_el, gran):
    lo = start_el // gran
    hi = (start_el + n_el - 1) // gran
    return [f"{prefix}{i}" for i in range(lo, hi + 1)]


def build_program():
    nc = bass.Bass("TRN2", target_bir_lowering=False)
    b = KB()

    x_in = nc.dram_tensor("x_in", [S, H], F32, kind="ExternalInput").ap()
    wq_d = nc.dram_tensor("wq", [H, H], F32, kind="ExternalInput").ap()
    wv_d = nc.dram_tensor("wv", [H, H], F32, kind="ExternalInput").ap()
    bq_d = nc.dram_tensor("bq", [P, HC], F32, kind="ExternalInput").ap()
    bv_d = nc.dram_tensor("bv", [1, H], F32, kind="ExternalInput").ap()
    am_d = nc.dram_tensor("amat", [H, H], F32, kind="ExternalInput").ap()
    clipx_d = nc.dram_tensor("clipx", [1, 1], F32, kind="ExternalInput").ap()
    clipd_d = nc.dram_tensor("clipd", [1, 1], F32, kind="ExternalInput").ap()
    wg_d = nc.dram_tensor("wg", [II, H], F32, kind="ExternalInput").ap()
    wu_d = nc.dram_tensor("wu", [II, H], F32, kind="ExternalInput").ap()
    wd_d = nc.dram_tensor("wd", [H, II], F32, kind="ExternalInput").ap()
    out_d = nc.dram_tensor("out", [S, H], F32, kind="ExternalOutput").ap()

    wgq = nc.dram_tensor("wgq", [II, H], BF16).ap()
    wuq = nc.dram_tensor("wuq", [II, H], BF16).ap()
    wdq = nc.dram_tensor("wdq", [H, II], BF16).ap()
    wqb = nc.dram_tensor("wqb", [H, H], BF16).ap()
    wvb = nc.dram_tensor("wvb", [H, H], BF16).ap()
    xqd = [nc.dram_tensor(f"xqd{i}", [S, H], BF16).ap() for i in range(2)]
    scd = nc.dram_tensor("scd", [S, H], BF16).ap()
    h_d = nc.dram_tensor("h_d", [II, S], BF16).ap()
    lmx_d = nc.dram_tensor("lmx_d", [4, P], F32).ap()
    cci = [nc.dram_tensor(f"cci{i}", [1, 1], F32).ap() for i in range(4)]
    cco = [nc.dram_tensor(f"cco{i}", [8, 1], F32).ap() for i in range(4)]

    sb = nc.alloc_sbuf_tensor
    x_res = sb("x_res", [P, NT * H], F32)
    act_fm = sb("act_fm", [P, HC * S], BF16)
    aq = sb("aq", [P, 16384], BF16)
    av = sb("av", [P, 16384], BF16)
    ae = sb("ae", [P, 16384], BF16)
    fscr = sb("fscr", [P, 2048], F32)
    i8s = sb("i8s", [P, 2048], I8)
    st = sb("stats", [P, 320], F32)
    bvr = sb("bvr", [1, H], BF16)
    onesr = sb("onesr", [1, P], BF16)
    onesc = sb("onesc", [P, 1], BF16)
    onesf = sb("onesf", [1, P], F32)
    bqf = sb("bqf", [P, HC], F32)
    sc0 = sb("sc0", [1, 16], F32)
    lmrow = sb("lmrow", [1, P], F32)
    garow = sb("garow", [1, 8], F32)

    pb = [nc.alloc_psum_tensor(f"pb{i}", [P, 512], F32) for i in range(8)]

    C_MVP, C_MEAN, C_VAR, C_RSTD, C_C1, C_C2 = 0, 32, 48, 64, 80, 96
    C_XMAX, C_XMIN, C_D1, C_D2, C_LM = 112, 128, 144, 160, 176
    C_INORM, C_GE, C_G, C_GB = 192, 208, 224, 240
    C_LC, C_HM, C_HT = 256, 257, 258
    C_SCB, C_SHB, C_RS, C_CS, C_PW = 260, 264, 268, 272, 276

    S_CLX, S_CLD, S_SGX, S_SGD, S_T0, S_T1, S_ROW = 0, 1, 2, 3, 4, 5, 6

    def dma(q, out, in_, r=(), w=(), tag="", tp=False):
        if tp:
            return b.op(q, (lambda o_, i_: lambda e: e.dma_start(out=o_, in_=i_, transpose=True))(out, in_), r=r, w=w, dma=True, tag=tag)
        return b.op(q, (lambda o_, i_: lambda e: e.dma_start(out=o_, in_=i_))(out, in_), r=r, w=w, dma=True, tag=tag)

    def mm(out, lhsT, rhs, start, stop, r=(), w=(), tag=""):
        return b.op("tensor", (lambda o_, l_, rh_, s1, s2: lambda e: e.matmul(o_, l_, rh_, start=s1, stop=s2))(out, lhsT, rhs, start, stop), r=r, w=w, tag=tag)

    def act(out, in_, func, r=(), w=(), bias=0.0, scale=1.0, tag=""):
        return b.op("scalar", (lambda o_, i_, f_, b_, s_: lambda e: e.activation(out=o_, in_=i_, func=f_, bias=b_, scale=s_))(out, in_, func, bias, scale), r=r, w=w, tag=tag)

    def vcopy(out, in_, r=(), w=(), tag=""):
        return b.op("vector", (lambda o_, i_: lambda e: e.tensor_copy(out=o_, in_=i_))(out, in_), r=r, w=w, tag=tag)

    def vts(out, in0, s1, s2, op0, op1=None, r=(), w=(), tag=""):
        if op1 is None:
            return b.op("vector", (lambda o_, i_, a_, p0: lambda e: e.tensor_scalar(out=o_, in0=i_, scalar1=a_, scalar2=None, op0=p0))(out, in0, s1, op0), r=r, w=w, tag=tag)
        return b.op("vector", (lambda o_, i_, a_, b_, p0, p1: lambda e: e.tensor_scalar(out=o_, in0=i_, scalar1=a_, scalar2=b_, op0=p0, op1=p1))(out, in0, s1, s2, op0, op1), r=r, w=w, tag=tag)

    def vtt(out, in0, in1, op, r=(), w=(), tag=""):
        return b.op("vector", (lambda o_, i0, i1, p: lambda e: e.tensor_tensor(out=o_, in0=i0, in1=i1, op=p))(out, in0, in1, op), r=r, w=w, tag=tag)

    def vstt(out, in0, scalar, in1, op0, op1, r=(), w=(), tag=""):
        return b.op("vector", (lambda o_, i0, s_, i1, p0, p1: lambda e: e.scalar_tensor_tensor(out=o_, in0=i0, scalar=s_, in1=i1, op0=p0, op1=p1))(out, in0, scalar, in1, op0, op1), r=r, w=w, tag=tag)

    def vred(out, in_, op, absval=False, r=(), w=(), tag=""):
        return b.op("vector", (lambda o_, i_, p, a_: lambda e: e.tensor_reduce(out=o_, in_=i_, axis=AX.X, op=p, apply_absolute_value=a_))(out, in_, op, absval), r=r, w=w, tag=tag)

    def vrecip(out, in_, r=(), w=(), tag=""):
        return b.op("vector", (lambda o_, i_: lambda e: e.reciprocal(out=o_, in_=i_))(out, in_), r=r, w=w, tag=tag)

    def vmemset(ap_, val, w=(), tag=""):
        return b.op("vector", (lambda a_, v_: lambda e: e.memset(a_, v_))(ap_, val), w=w, tag=tag)

    def vbnstats(out, in_, r=(), w=(), tag=""):
        return b.op("vector", (lambda o_, i_: lambda e: e.bn_stats(out=o_, in_=i_))(out, in_), r=r, w=w, tag=tag)

    def vbnaggr(out, in_, r=(), w=(), tag=""):
        return b.op("vector", (lambda o_, i_: lambda e: e.bn_aggr(out=o_, in_=i_))(out, in_), r=r, w=w, tag=tag)

    def xt(t):
        return x_res[:, t * H:(t + 1) * H]

    for t in range(NT):
        dma("sync", xt(t), x_in[t * P:(t + 1) * P, :], w=[f"X{t}"], tag=f"ldx{t}")
    dma("sync", sc0[:, S_CLX:S_CLX + 1], clipx_d, w=["SC_clx"], tag="ldclx")
    dma("sync", sc0[:, S_CLD:S_CLD + 1], clipd_d, w=["SC_cld"], tag="ldcld")
    act(sc0[:, S_SGX:S_SGX + 1], sc0[:, S_CLX:S_CLX + 1], AF.Sigmoid,
        r=["SC_clx"], w=["SC_sgx"], tag="sigx")
    act(sc0[:, S_SGD:S_SGD + 1], sc0[:, S_CLD:S_CLD + 1], AF.Sigmoid,
        r=["SC_cld"], w=["SC_sgd"], tag="sigd")
    b.op("gpsimd", lambda e: e.dma_start(out=bvr[:], in_=bv_d), w=["BV"], dma=True, tag="ldbv")
    dma("sync", bqf[:], bq_d, w=["BQF"], tag="ldbq")
    vmemset(onesr[:], 1.0, w=["ONESR"], tag="onesr")
    vmemset(onesc[:], 1.0, w=["ONESC"], tag="onesc")
    vmemset(onesf[:], 1.0, w=["ONESF"], tag="onesf")
    for oc in range(HC):
        b.op("gpsimd",
             (lambda o_, i_: lambda e: e.dma_start(out=o_, in_=i_))(
                 ae[:, 8192 + oc * 1024:8192 + (oc + 1) * 1024],
                 am_d[oc * P:(oc + 1) * P, :]),
             w=_regs("AE", 8192 + oc * 1024, 1024, 4096), dma=True, tag=f"ldA{oc}")

    def prep_gu(w_src, w_dst, dkey, r_tile):
        par = r_tile % 2
        fs = fscr[:, par * 1024:(par + 1) * 1024]
        fk = _regs("FS", par * 1024, 1024, 512)
        i8 = i8s[:, par * 1024:(par + 1) * 1024]
        ik = _regs("I8", par * 1024, 1024, 512)
        avs = av[:, par * 1024:(par + 1) * 1024]
        avk = _regs("AV", par * 1024, 1024, 512)
        pw = st[:, C_PW + 3 * par:C_PW + 3 * par + 3]
        pk = [f"STpw{par}"]
        dma("scalar", fs, w_src[r_tile * P:(r_tile + 1) * P, :], w=fk, tag=f"pl{dkey}{r_tile}")
        vred(pw[:, 0:1], fs, ALU.max, absval=True, r=fk, w=pk)
        vrecip(pw[:, 1:2], pw[:, 0:1], r=pk, w=pk)
        vts(pw[:, 1:2], pw[:, 1:2], QMAX, None, ALU.mult, r=pk, w=pk)
        vts(pw[:, 2:3], pw[:, 0:1], 1.0 / QMAX, None, ALU.mult, r=pk, w=pk)
        act(fs, fs, AF.Identity, scale=pw[:, 1:2], r=fk + pk, w=fk)
        vcopy(i8, fs, r=fk, w=ik)
        act(avs, i8, AF.Identity, scale=pw[:, 2:3], r=ik + pk, w=avk)
        dma("scalar", w_dst[r_tile * P:(r_tile + 1) * P, :], avs, r=avk,
            w=[f"{dkey}{r_tile}"], tag=f"ps{dkey}{r_tile}")

    def prep_wd(r_tile):
        HF = II // 2
        fs = fscr[:, 0:HF]
        fk = ["FS0", "FS1", "FS2"]
        pw = st[:, C_PW + 6:C_PW + 12]
        pk = ["STpwd"]
        rows = wd_d[r_tile * P:(r_tile + 1) * P, :]
        dma("scalar", fs, rows[:, 0:HF], w=fk, tag=f"plwd{r_tile}a")
        vred(pw[:, 0:1], fs, ALU.max, absval=True, r=fk, w=pk)
        dma("scalar", fs, rows[:, HF:II], w=fk, tag=f"plwd{r_tile}b")
        vred(pw[:, 1:2], fs, ALU.max, absval=True, r=fk, w=pk)
        vtt(pw[:, 0:1], pw[:, 0:1], pw[:, 1:2], ALU.max, r=pk, w=pk)
        vrecip(pw[:, 2:3], pw[:, 0:1], r=pk, w=pk)
        vts(pw[:, 2:3], pw[:, 2:3], QMAX, None, ALU.mult, r=pk, w=pk)
        vts(pw[:, 3:4], pw[:, 0:1], 1.0 / QMAX, None, ALU.mult, r=pk, w=pk)
        for hf in range(2):
            i8 = i8s[:, 0:HF]
            ik = _regs("I8", 0, HF, 512)
            avs = av[:, 0:HF]
            avk = _regs("AV", 0, HF, 512)
            dma("scalar", fs, rows[:, hf * HF:(hf + 1) * HF], w=fk, tag=f"plwd{r_tile}c{hf}")
            act(fs, fs, AF.Identity, scale=pw[:, 2:3], r=fk + pk, w=fk)
            vcopy(i8, fs, r=fk, w=ik)
            act(avs, i8, AF.Identity, scale=pw[:, 3:4], r=ik + pk, w=avk)
            dma("scalar", wdq[r_tile * P:(r_tile + 1) * P, hf * HF:(hf + 1) * HF],
                avs, r=avk, w=[f"WDQ{r_tile}"], tag=f"pswd{r_tile}{hf}")

    def prep_attw(w_src, w_dst, dkey, r_tile):
        par = r_tile % 2
        avs = av[:, 2048 + par * 1024:2048 + (par + 1) * 1024]
        avk = _regs("AV", 2048 + par * 1024, 1024, 512)
        b.op("gpsimd",
             (lambda o_, i_: lambda e: e.dma_start(out=o_, in_=i_))(
                 avs, w_src[r_tile * P:(r_tile + 1) * P, :]),
             w=avk, dma=True, tag=f"pc{dkey}{r_tile}")
        dma("sync", w_dst[r_tile * P:(r_tile + 1) * P, :], avs, r=avk,
            w=[f"{dkey}{r_tile}"], tag=f"pw{dkey}{r_tile}")

    for r_ in range(ICN):
        prep_gu(wg_d, wgq, "WGQ", r_)
        prep_gu(wu_d, wuq, "WUQ", r_)

    prep_tasks = [(lambda rr=r_: prep_wd(rr)) for r_ in range(HC)]
    attw_tasks = ([(lambda rr=r_: prep_attw(wq_d, wqb, "WQB", rr)) for r_ in range(HC)]
                  + [(lambda rr=r_: prep_attw(wv_d, wvb, "WVB", rr)) for r_ in range(HC)])

    def ln_stats(phase, want_minmax):
        mvp = st[:, C_MVP:C_MVP + 32]
        for t in range(NT):
            vbnstats(st[:, 300:306], xt(t)[:, 0:512], r=[f"X{t}"], w=["STb6"])
            vbnstats(st[:, 306:312], xt(t)[:, 512:1024], r=[f"X{t}"], w=["STb6"])
            vbnaggr(mvp[:, 2 * t:2 * t + 2], st[:, 300:312], r=["STb6"], w=["STmvp"])
            if want_minmax:
                vred(st[:, C_XMAX + t:C_XMAX + t + 1], xt(t), ALU.max,
                     r=[f"X{t}"], w=["STxmax"])
                vred(st[:, C_XMIN + t:C_XMIN + t + 1], xt(t), ALU.min,
                     r=[f"X{t}"], w=["STxmin"])
        vcopy(st[:, C_MEAN:C_MEAN + 16], mvp[:, 0:32:2], r=["STmvp"], w=["STmean"])
        vcopy(st[:, C_VAR:C_VAR + 16], mvp[:, 1:32:2], r=["STmvp"], w=["STvar"])
        vts(st[:, C_RSTD:C_RSTD + 16], st[:, C_VAR:C_VAR + 16], EPS_LN, None,
            ALU.add, r=["STvar"], w=["STrstd"])
        act(st[:, C_RSTD:C_RSTD + 16], st[:, C_RSTD:C_RSTD + 16], AF.Sqrt,
            r=["STrstd"], w=["STrstd"])
        vrecip(st[:, C_RSTD:C_RSTD + 16], st[:, C_RSTD:C_RSTD + 16],
               r=["STrstd"], w=["STrstd"])

    def global_scale(k, lmax_src_key, sig_slot, sig_key, fold, cdst, ckey):
        dma("gpsimd", cci[k], sc0[:, S_T0:S_T0 + 1], r=[lmax_src_key],
            w=[f"CCI{k}"], tag=f"cci{k}")
        b.op("gpsimd",
             (lambda i_, o_: lambda e: e.collective_compute(
                 "AllGather", ALU.bypass, replica_groups=[list(range(8))],
                 ins=[i_], outs=[o_]))(cci[k], cco[k]),
             r=[f"CCI{k}"], w=[f"CCO{k}"], coll=True, tag=f"cc{k}")
        dma("gpsimd", garow[:], cco[k].rearrange("a b -> b a"), r=[f"CCO{k}"],
            w=["GAROW"], tag=f"ga{k}")
        vred(sc0[:, S_T1:S_T1 + 1], garow[:], ALU.max, r=["GAROW"], w=["SC_t1"])
        vtt(sc0[:, S_T1:S_T1 + 1], sc0[:, S_T1:S_T1 + 1], sc0[:, sig_slot:sig_slot + 1],
            ALU.mult, r=["SC_t1", sig_key], w=["SC_t1"])
        vts(sc0[:, S_T1:S_T1 + 1], sc0[:, S_T1:S_T1 + 1], 1.0 / QMAX, None,
            ALU.mult, r=["SC_t1"], w=["SC_t1"])
        vrecip(sc0[:, S_ROW + 1:S_ROW + 2], sc0[:, S_T1:S_T1 + 1],
               r=["SC_t1"], w=["SC_row"])
        vts(sc0[:, S_ROW:S_ROW + 1], sc0[:, S_T1:S_T1 + 1], fold, None,
            ALU.mult, r=["SC_t1"], w=["SC_row"])
        mm(pb[6][:, 0:2], onesf[0:1, :], sc0[0:1, S_ROW:S_ROW + 2], True, True,
           r=["ONESF", "SC_row"], w=["PB6"], tag=f"bc{k}")
        vcopy(st[:, cdst:cdst + 2], pb[6][:, 0:2], r=["PB6"], w=[ckey])

    def local_lnmax(k):
        c = st
        vtt(c[:, C_D1:C_D1 + 16], c[:, C_XMAX:C_XMAX + 16], c[:, C_MEAN:C_MEAN + 16],
            ALU.subtract, r=["STxmax", "STmean"], w=["STd1"])
        vtt(c[:, C_D2:C_D2 + 16], c[:, C_MEAN:C_MEAN + 16], c[:, C_XMIN:C_XMIN + 16],
            ALU.subtract, r=["STmean", "STxmin"], w=["STd2"])
        vtt(c[:, C_D1:C_D1 + 16], c[:, C_D1:C_D1 + 16], c[:, C_D2:C_D2 + 16],
            ALU.max, r=["STd1", "STd2"], w=["STd1"])
        vtt(c[:, C_LM:C_LM + 16], c[:, C_D1:C_D1 + 16], c[:, C_RSTD:C_RSTD + 16],
            ALU.mult, r=["STd1", "STrstd"], w=["STlm"])
        vred(c[:, C_LC:C_LC + 1], c[:, C_LM:C_LM + 16], ALU.max, r=["STlm"], w=["STlc"])
        dma("sync", lmx_d[k:k + 1, :].rearrange("a b -> b a"), c[:, C_LC:C_LC + 1],
            r=["STlc"], w=[f"LMX{k}"], tag=f"lmo{k}")
        dma("sync", lmrow[:], lmx_d[k:k + 1, :], r=[f"LMX{k}"], w=["LMROW"], tag=f"lmi{k}")
        vred(sc0[:, S_T0:S_T0 + 1], lmrow[:], ALU.max, r=["LMROW"], w=["SC_t0"])

    def ffn(call):
        k_ln, k_h = 2 * call, 2 * call + 1
        xq_d = xqd[call]
        ln_stats("f", True)
        local_lnmax(k_ln)
        global_scale(k_ln, "SC_t0", S_SGX, "SC_sgx", 1.0, C_SCB, "STscb")
        vts(st[:, C_C1:C_C1 + 16], st[:, C_RSTD:C_RSTD + 16],
            st[:, C_SCB + 1:C_SCB + 2], None, ALU.mult,
            r=["STrstd", "STscb"], w=["STc1"])
        vstt(st[:, C_C2:C_C2 + 16], st[:, C_MEAN:C_MEAN + 16], -1.0,
             st[:, C_C1:C_C1 + 16], ALU.mult, ALU.mult,
             r=["STmean", "STc1"], w=["STc2"])
        for t in range(NT):
            par = t % 2
            fs = fscr[:, par * 1024:(par + 1) * 1024]
            fk = _regs("FS", par * 1024, 1024, 512)
            i8 = i8s[:, par * 1024:(par + 1) * 1024]
            ik = _regs("I8", par * 1024, 1024, 512)
            avs = av[:, par * 1024:(par + 1) * 1024]
            avk = _regs("AV", par * 1024, 1024, 512)
            act(fs, xt(t), AF.Identity, scale=st[:, C_C1 + t:C_C1 + t + 1],
                bias=st[:, C_C2 + t:C_C2 + t + 1],
                r=[f"X{t}", "STc1", "STc2"], w=fk, tag=f"xqa{t}")
            vcopy(i8, fs, r=fk, w=ik)
            act(avs, i8, AF.Identity, scale=st[:, C_SCB:C_SCB + 1],
                r=ik + ["STscb"], w=avk, tag=f"xqd{t}")
            dma("scalar", xq_d[t * P:(t + 1) * P, :], avs, r=avk,
                w=[f"XQ{call}_{t // 4}"], tag=f"xqs{t}")
        for hc in range(HC):
            for q4 in range(4):
                dma("sync", act_fm[:, hc * S + q4 * 512:hc * S + (q4 + 1) * 512],
                    xq_d[q4 * 512:(q4 + 1) * 512, hc * P:(hc + 1) * P],
                    r=[f"XQ{call}_{q4}"], w=[f"AF{hc}q{q4}"], tp=True,
                    tag=f"xqt{hc}_{q4}")
        vmemset(st[:, C_HM:C_HM + 1], 0.0, w=["SThm"], tag="hm0")
        ibs = [(0, 4), (4, 4), (8, 4), (12, 4), (16, 4), (20, 2)]
        it = 0
        for ibi, (ic0, nic) in enumerate(ibs):
            par = ibi % 2
            for wti, wsrc in enumerate((wgq, wuq)):
                base = (wti * 2 + par) * 4096
                for hc in range(HC):
                    dst = aq[:, base + hc * nic * P: base + (hc + 1) * nic * P]
                    dma("sync", dst, wsrc[ic0 * P:(ic0 + nic) * P, hc * P:(hc + 1) * P],
                        r=[f"{'WGQ' if wti == 0 else 'WUQ'}{ic0 + j}" for j in range(nic)],
                        w=_regs("AQ", base + hc * nic * P, nic * P, 2048), tp=True,
                        tag=f"g{call}w{ibi}_{wti}_{hc}")
            for icw in range(nic):
                ic = ic0 + icw
                for sbk in range(4):
                    gbank, ubank = it % 3, 3 + it % 3
                    it += 1
                    for wti, bank in ((0, gbank), (1, ubank)):
                        base = (wti * 2 + par) * 4096
                        rk = _regs("AQ", base, 4096, 2048)
                        for hc in range(HC):
                            lhsT = aq[:, base + hc * nic * P + icw * P:
                                      base + hc * nic * P + (icw + 1) * P]
                            rhs = act_fm[:, hc * S + sbk * 512:hc * S + (sbk + 1) * 512]
                            mm(pb[bank][:, :], lhsT, rhs, hc == 0, hc == HC - 1,
                               r=rk + [f"AF{hc}q{sbk}"], w=[f"PB{bank}"],
                               tag=f"g{call}m{ic}_{sbk}_{wti}")
                    p2 = it % 2
                    sil = av[:, 4096 + p2 * 512:4096 + (p2 + 1) * 512]
                    silk = _regs("AV", 4096 + p2 * 512, 512, 512)
                    hst = av[:, 5120 + p2 * 512:5120 + (p2 + 1) * 512]
                    hk = _regs("AV", 5120 + p2 * 512, 512, 512)
                    act(sil, pb[gbank][:, :], AF.Silu, r=[f"PB{gbank}"], w=silk,
                        tag=f"sil{call}_{ic}_{sbk}")
                    vtt(hst, sil, pb[ubank][:, :], ALU.mult,
                        r=silk + [f"PB{ubank}"], w=hk, tag=f"hmul{call}_{ic}_{sbk}")
                    vred(st[:, C_HT:C_HT + 1], hst, ALU.max, absval=True,
                         r=hk, w=["STht"])
                    vtt(st[:, C_HM:C_HM + 1], st[:, C_HM:C_HM + 1],
                        st[:, C_HT:C_HT + 1], ALU.max, r=["SThm", "STht"], w=["SThm"])
                    dma("gpsimd", h_d[ic * P:(ic + 1) * P, sbk * 512:(sbk + 1) * 512],
                        hst, r=hk, w=[f"HD{ic}_{sbk}"], tag=f"hs{call}_{ic}_{sbk}")
            if call == 0:
                for _ in range(2):
                    if prep_tasks:
                        prep_tasks.pop(0)()
        vcopy(st[:, C_LC:C_LC + 1], st[:, C_HM:C_HM + 1], r=["SThm"], w=["STlc"])
        dma("sync", lmx_d[k_h:k_h + 1, :].rearrange("a b -> b a"),
            st[:, C_LC:C_LC + 1], r=["STlc"], w=[f"LMX{k_h}"], tag=f"lmo{k_h}")
        dma("sync", lmrow[:], lmx_d[k_h:k_h + 1, :], r=[f"LMX{k_h}"], w=["LMROW"],
            tag=f"lmi{k_h}")
        vred(sc0[:, S_T0:S_T0 + 1], lmrow[:], ALU.max, r=["LMROW"], w=["SC_t0"])
        global_scale(k_h, "SC_t0", S_SGD, "SC_sgd", 0.5, C_SHB, "STshb")
        for g in range(4):
            for ic in range(ICN):
                par = ic % 2
                hin = av[:, 6144 + par * 512:6144 + (par + 1) * 512]
                hink = _regs("AV", 6144 + par * 512, 512, 512)
                dma("scalar", hin, h_d[ic * P:(ic + 1) * P, g * 512:(g + 1) * 512],
                    r=[f"HD{ic}_{g}"], w=hink, tag=f"hl{call}_{g}_{ic}")
                fs = fscr[:, par * 512:(par + 1) * 512]
                fk = [f"FS{par}"]
                act(fs, hin, AF.Identity, scale=st[:, C_SHB + 1:C_SHB + 2],
                    r=hink + ["STshb"], w=fk, tag=f"hqa{call}_{g}_{ic}")
                i8 = i8s[:, par * 512:(par + 1) * 512]
                ik = _regs("I8", par * 512, 512, 512)
                vcopy(i8, fs, r=fk, w=ik)
                hq = av[:, 7168 + par * 512:7168 + (par + 1) * 512]
                hqk = _regs("AV", 7168 + par * 512, 512, 512)
                act(hq, i8, AF.Identity, scale=st[:, C_SHB:C_SHB + 1],
                    r=ik + ["STshb"], w=hqk, tag=f"hqd{call}_{g}_{ic}")
                wbase = 8192 + par * 2048
                wds = aq[:, wbase:wbase + 1024]
                wdk = _regs("AQ", wbase, 1024, 2048)
                dma("sync", wds, wdq[0:H, ic * P:(ic + 1) * P],
                    r=[f"WDQ{rr}" for rr in range(HC)], w=wdk, tp=True,
                    tag=f"wdl{call}_{g}_{ic}")
                for st4 in range(4):
                    for hb in range(2):
                        bank = st4 * 2 + hb
                        mm(pb[bank][:, :], hq[:, st4 * P:(st4 + 1) * P],
                           wds[:, hb * 512:(hb + 1) * 512], ic == 0, ic == ICN - 1,
                           r=hqk + wdk, w=[f"PB{bank}"], tag=f"dm{call}_{g}_{ic}")
            for st4 in range(4):
                t = g * 4 + st4
                for hb in range(2):
                    bank = st4 * 2 + hb
                    vtt(xt(t)[:, hb * 512:(hb + 1) * 512], pb[bank][:, :],
                        xt(t)[:, hb * 512:(hb + 1) * 512], ALU.add,
                        r=[f"PB{bank}", f"X{t}"], w=[f"X{t}"], tag=f"dres{call}_{t}_{hb}")
            if call == 1 and g == 0:
                pass
            if call == 0:
                for _ in range(4):
                    if attw_tasks:
                        attw_tasks.pop(0)()

    def attn():
        ln_stats("a", False)
        c = st
        vts(c[:, C_INORM:C_INORM + 16], c[:, C_VAR:C_VAR + 16], float(H), None,
            ALU.mult, r=["STvar"], w=["STin"])
        act(c[:, C_INORM:C_INORM + 16], c[:, C_INORM:C_INORM + 16], AF.Sqrt,
            r=["STin"], w=["STin"])
        vtt(c[:, C_INORM:C_INORM + 16], c[:, C_INORM:C_INORM + 16],
            c[:, C_RSTD:C_RSTD + 16], ALU.mult, r=["STin", "STrstd"], w=["STin"])
        vts(c[:, C_GE:C_GE + 16], c[:, C_INORM:C_INORM + 16], EPS_NRM, None,
            ALU.add, r=["STin"], w=["STge"])
        vrecip(c[:, C_GE:C_GE + 16], c[:, C_GE:C_GE + 16], r=["STge"], w=["STge"])
        vtt(c[:, C_G:C_G + 16], c[:, C_GE:C_GE + 16], c[:, C_RSTD:C_RSTD + 16],
            ALU.mult, r=["STge", "STrstd"], w=["STg"])
        vstt(c[:, C_GB:C_GB + 16], c[:, C_MEAN:C_MEAN + 16], -1.0,
             c[:, C_G:C_G + 16], ALU.mult, ALU.mult,
             r=["STmean", "STg"], w=["STgb"])
        for t in range(NT):
            par = t % 2
            avs = av[:, par * 1024:(par + 1) * 1024]
            avk = _regs("AV", par * 1024, 1024, 512)
            act(avs, xt(t), AF.Identity, scale=c[:, C_G + t:C_G + t + 1],
                bias=c[:, C_GB + t:C_GB + t + 1],
                r=[f"X{t}", "STg", "STgb"], w=avk, tag=f"sca{t}")
            dma("scalar", scd[t * P:(t + 1) * P, :], avs, r=avk,
                w=[f"SCD{t // 4}"], tag=f"scs{t}")
        for hc in range(HC):
            for q4 in range(4):
                dma("sync", act_fm[:, hc * S + q4 * 512:hc * S + (q4 + 1) * 512],
                    scd[q4 * 512:(q4 + 1) * 512, hc * P:(hc + 1) * P],
                    r=[f"SCD{q4}"], w=[f"AF{hc}q{q4}"], tp=True, tag=f"sct{hc}_{q4}")
        for hc in range(HC):
            dma("sync", ae[:, hc * 1024:(hc + 1) * 1024],
                wqb[0:H, hc * P:(hc + 1) * P], r=[f"WQB{rr}" for rr in range(HC)],
                w=_regs("AE", hc * 1024, 1024, 4096), tp=True, tag=f"wqT{hc}")
        for sbk in range(4):
            for oc in range(HC):
                bank = oc
                for hc in range(HC):
                    mm(pb[bank][:, :], ae[:, hc * 1024 + oc * P:hc * 1024 + (oc + 1) * P],
                       act_fm[:, hc * S + sbk * 512:hc * S + (sbk + 1) * 512],
                       hc == 0, hc == HC - 1,
                       r=_regs("AE", hc * 1024, 1024, 4096) + [f"AF{hc}q{sbk}"],
                       w=[f"PB{bank}"], tag=f"qm{oc}_{sbk}")
                act(aq[:, oc * S + sbk * 512:oc * S + (sbk + 1) * 512], pb[bank][:, :],
                    AF.Identity, bias=bqf[:, oc:oc + 1],
                    r=[f"PB{bank}", "BQF"],
                    w=_regs("AQ", oc * S + sbk * 512, 512, 2048), tag=f"qe{oc}_{sbk}")
        for hc in range(HC):
            dma("sync", ae[:, hc * 1024:(hc + 1) * 1024],
                wvb[0:H, hc * P:(hc + 1) * P], r=[f"WVB{rr}" for rr in range(HC)],
                w=_regs("AE", hc * 1024, 1024, 4096), tp=True, tag=f"wvT{hc}")
        for tt_ in range(NT):
            for ob in range(2):
                bank = (2 * tt_ + ob) % 8
                for hc in range(HC):
                    mm(pb[bank][:, :],
                       act_fm[:, hc * S + tt_ * P:hc * S + (tt_ + 1) * P],
                       ae[:, hc * 1024 + ob * 512:hc * 1024 + (ob + 1) * 512],
                       hc == 0, False,
                       r=_regs("AE", hc * 1024, 1024, 4096) + [f"AF{hc}q{tt_ // 4}"],
                       w=[f"PB{bank}"], tag=f"vm{tt_}_{ob}")
                mm(pb[bank][:, :], onesr[0:1, :], bvr[0:1, ob * 512:(ob + 1) * 512],
                   False, True, r=["ONESR", "BV"], w=[f"PB{bank}"], tag=f"vb{tt_}_{ob}")
                vcopy(av[:, tt_ * 1024 + ob * 512:tt_ * 1024 + (ob + 1) * 512],
                      pb[bank][:, :], r=[f"PB{bank}"],
                      w=_regs("AV", tt_ * 1024 + ob * 512, 512, 512),
                      tag=f"ve{tt_}_{ob}")
        for sbk in range(4):
            for kc in range(HC):
                bank = kc
                for oc in range(HC):
                    lhsT = ae[:, 8192 + oc * 1024 + kc * P:8192 + oc * 1024 + (kc + 1) * P]
                    rhs = aq[:, oc * S + sbk * 512:oc * S + (sbk + 1) * 512]
                    mm(pb[bank][:, :], lhsT, rhs, oc == 0, oc == HC - 1,
                       r=_regs("AE", 8192 + oc * 1024, 1024, 4096)
                       + _regs("AQ", oc * S + sbk * 512, 512, 2048),
                       w=[f"PB{bank}"], tag=f"tqm{kc}_{sbk}")
                act(act_fm[:, kc * S + sbk * 512:kc * S + (sbk + 1) * 512],
                    pb[bank][:, :], AF.Identity, r=[f"PB{bank}"],
                    w=[f"AF{kc}q{sbk}"], tag=f"tqe{kc}_{sbk}")
        for sbk in range(4):
            ebase = (sbk % 2) * 8192
            ek = lambda tc: _regs("AE", ebase + tc * 512, 512, 4096)
            for tc in range(NT):
                bank = tc % 3
                for oc in range(HC):
                    mm(pb[bank][:, :],
                       act_fm[:, oc * S + tc * P:oc * S + (tc + 1) * P],
                       aq[:, oc * S + sbk * 512:oc * S + (sbk + 1) * 512],
                       oc == 0, oc == HC - 1,
                       r=[f"AF{oc}q{tc // 4}"]
                       + _regs("AQ", oc * S + sbk * 512, 512, 2048),
                       w=[f"PB{bank}"], tag=f"sm{sbk}_{tc}")
                act(ae[:, ebase + tc * 512:ebase + (tc + 1) * 512], pb[bank][:, :],
                    AF.Exp, scale=1.0 / 32.0, r=[f"PB{bank}"], w=ek(tc),
                    tag=f"ex{sbk}_{tc}")
            for st4 in range(4):
                for tc in range(NT):
                    mm(pb[7][:, st4:st4 + 1],
                       ae[:, ebase + tc * 512 + st4 * P:ebase + tc * 512 + (st4 + 1) * P],
                       onesc[:, :], tc == 0, tc == NT - 1,
                       r=ek(tc) + ["ONESC"], w=["PB7"], tag=f"rs{sbk}_{st4}")
            vrecip(st[:, C_RS:C_RS + 4], pb[7][:, 0:4], r=["PB7"], w=["STrs"])
            vtt(st[:, C_CS:C_CS + 4], st[:, C_RS:C_RS + 4],
                st[:, C_INORM + sbk * 4:C_INORM + sbk * 4 + 4], ALU.mult,
                r=["STrs", "STin"], w=["STcs"])
            for ob in range(2):
                for st4 in range(4):
                    bank = 3 + st4
                    for tc in range(NT):
                        mm(pb[bank][:, :],
                           ae[:, ebase + tc * 512 + st4 * P:ebase + tc * 512 + (st4 + 1) * P],
                           av[:, tc * 1024 + ob * 512:tc * 1024 + (ob + 1) * 512],
                           tc == 0, tc == NT - 1,
                           r=ek(tc) + _regs("AV", tc * 1024 + ob * 512, 512, 512),
                           w=[f"PB{bank}"], tag=f"cm{sbk}_{ob}_{st4}")
                for st4 in range(4):
                    bank = 3 + st4
                    t = sbk * 4 + st4
                    vstt(xt(t)[:, ob * 512:(ob + 1) * 512], pb[bank][:, :],
                         st[:, C_CS + st4:C_CS + st4 + 1],
                         xt(t)[:, ob * 512:(ob + 1) * 512], ALU.mult, ALU.add,
                         r=[f"PB{bank}", f"X{t}", "STcs"], w=[f"X{t}"],
                         tag=f"ce{sbk}_{ob}_{st4}")

    if RUN_FFN1:
        ffn(0)
    while prep_tasks:
        prep_tasks.pop(0)()
    while attw_tasks:
        attw_tasks.pop(0)()
    if RUN_ATTN:
        attn()
    if RUN_FFN3:
        ffn(1)

    for t in range(NT):
        dma("gpsimd", out_d[t * P:(t + 1) * P, :], xt(t), r=[f"X{t}"],
            w=[f"OUT{t}"], tag=f"sto{t}")

    counts = b.finalize(nc)
    return nc, counts


_CACHE = {}


def kernel(L0_attn_hidden_states, L0_attn_Wq, L0_attn_bq, L0_attn_Wv, L0_attn_bv,
           L0_attn_anomaly_matrix, L0_ffn_clip_x, L0_ffn_clip_down,
           L0_ffn_w_gate, L0_ffn_w_up, L0_ffn_w_down):
    hs = np.asarray(L0_attn_hidden_states, dtype=np.float32)
    nb = hs.shape[0]
    common = {
        "wq": np.ascontiguousarray(np.asarray(L0_attn_Wq, np.float32)),
        "wv": np.ascontiguousarray(np.asarray(L0_attn_Wv, np.float32)),
        "bq": np.ascontiguousarray(np.asarray(L0_attn_bq, np.float32).reshape(HC, P).T),
        "bv": np.ascontiguousarray(np.asarray(L0_attn_bv, np.float32).reshape(1, H)),
        "amat": np.ascontiguousarray(np.asarray(L0_attn_anomaly_matrix, np.float32)),
        "clipx": np.ascontiguousarray(np.asarray(L0_ffn_clip_x, np.float32).reshape(1, 1)),
        "clipd": np.ascontiguousarray(np.asarray(L0_ffn_clip_down, np.float32).reshape(1, 1)),
        "wg": np.ascontiguousarray(np.asarray(L0_ffn_w_gate, np.float32)),
        "wu": np.ascontiguousarray(np.asarray(L0_ffn_w_up, np.float32)),
        "wd": np.ascontiguousarray(np.asarray(L0_ffn_w_down, np.float32)),
    }
    if "nc" not in _CACHE:
        _CACHE["nc"] = build_program()
    nc, _ = _CACHE["nc"]
    in_maps = [dict(common, x_in=np.ascontiguousarray(hs[b_])) for b_ in range(nb)]
    res = run_bass_kernel_spmd(nc, in_maps, core_ids=list(range(nb)))
    out = np.stack([np.asarray(res.results[b_]["out"]) for b_ in range(nb)], axis=0)
    return out.astype(np.float32)


if __name__ == "__main__":
    nc, counts = build_program()
    print("per-engine sem counts:", counts)
    print("n_instructions:", nc.n_instructions())
    print("sbuf_bytes_remaining:", nc.sbuf_bytes_remaining)


# version 04
# speedup vs baseline: 1.0703x; 1.0703x over previous
import numpy as np
import concourse.bass as bass
import concourse.mybir as mybir
from concourse.bass_utils import run_bass_kernel_spmd

F32 = mybir.dt.float32
BF16 = mybir.dt.bfloat16
I8 = mybir.dt.int8
AF = mybir.ActivationFunctionType
ALU = mybir.AluOpType
AX = mybir.AxisListType

P = 128
S = 2048
H = 1024
II = 2816
NT = S // P
HC = H // P
ICN = II // P
QMAX = 127.0
EPS_LN = 1e-5
EPS_NRM = 1e-9

ENGS = ("sync", "gpsimd", "scalar", "vector", "tensor")
DMA_QUEUES = ("sync", "gpsimd", "scalar")

RUN_FFN1 = True
RUN_ATTN = True
RUN_FFN3 = True


class Op:
    __slots__ = ("eng", "fn", "deps", "dma", "inc", "needs_inc", "count_after",
                 "gidx", "tag")

    def __init__(self, eng, fn, deps, dma, gidx, tag, inc=0):
        self.eng = eng
        self.fn = fn
        self.deps = deps
        self.dma = dma
        self.inc = inc
        self.needs_inc = False
        self.count_after = 0
        self.gidx = gidx
        self.tag = tag


class KB:

    def __init__(self):
        self.per_eng = {e: [] for e in ENGS}
        self.writer = {}
        self.readers = {}
        self.n = 0

    def op(self, eng, fn, r=(), w=(), dma=False, coll=False, tag=""):
        deps = set()
        rr, ww = list(r), list(w)
        rr2 = []
        for res in rr:
            if res.startswith("PB"):
                ww.append(res)
            else:
                rr2.append(res)
        for res in rr2:
            wop = self.writer.get(res)
            if wop is not None:
                deps.add(wop)
        for res in ww:
            wop = self.writer.get(res)
            if wop is not None:
                deps.add(wop)
            for rop in self.readers.get(res, {}).values():
                deps.add(rop)
        o = Op(eng, fn, deps, dma, self.n, tag)
        o.deps.discard(o)
        self.n += 1
        for res in rr2:
            self.readers.setdefault(res, {})[eng] = o
        for res in ww:
            self.writer[res] = o
            self.readers[res] = {}
        self.per_eng[eng].append(o)
        if coll:
            o.inc = 1
            o.needs_inc = True
        return o

    def finalize(self, nc):
        for eng in ENGS:
            for o in self.per_eng[eng]:
                for d in o.deps:
                    if d.eng == "tensor" and o.eng == "tensor":
                        continue
                    d.needs_inc = True
        counts = {}
        for eng in ENGS:
            c = 0
            for o in self.per_eng[eng]:
                if o.dma:
                    o.inc = 16
                elif o.needs_inc and o.inc == 0:
                    o.inc = 1
                c += o.inc
                o.count_after = c
            counts[eng] = c
        sems = {e: nc.alloc_semaphore(name="sem_" + e) for e in ENGS}
        waited = {}

        def body_for(engname):
            def body(proxy):
                for o in self.per_eng[engname]:
                    byeng = {}
                    for d in o.deps:
                        if d.eng == "tensor" and engname == "tensor":
                            continue
                        assert d.gidx < o.gidx, (d.tag, o.tag)
                        assert d.inc > 0, (d.tag, o.tag)
                        if d.count_after > byeng.get(d.eng, 0):
                            byeng[d.eng] = d.count_after
                    for deng in sorted(byeng):
                        v = byeng[deng]
                        key = (engname, deng)
                        if waited.get(key, 0) >= v:
                            continue
                        proxy.wait_ge(sems[deng], v)
                        waited[key] = v
                    inst = o.fn(proxy)
                    if o.inc:
                        inst.then_inc(sems[o.eng], o.inc)
                if engname in DMA_QUEUES and counts[engname] > 0:
                    proxy.wait_ge(sems[engname], counts[engname])
            return body

        with nc.Block() as block:
            block.sync(body_for("sync"))
            block.gpsimd(body_for("gpsimd"))
            block.scalar(body_for("scalar"))
            block.vector(body_for("vector"))
            block.tensor(body_for("tensor"))
        return counts


def _regs(prefix, start_el, n_el, gran):
    lo = start_el // gran
    hi = (start_el + n_el - 1) // gran
    return [f"{prefix}{i}" for i in range(lo, hi + 1)]


def build_program():
    nc = bass.Bass("TRN2", target_bir_lowering=False)
    b = KB()

    x_in = nc.dram_tensor("x_in", [S, H], F32, kind="ExternalInput").ap()
    wq_d = nc.dram_tensor("wq", [H, H], F32, kind="ExternalInput").ap()
    wv_d = nc.dram_tensor("wv", [H, H], F32, kind="ExternalInput").ap()
    bq_d = nc.dram_tensor("bq", [P, HC], F32, kind="ExternalInput").ap()
    bv_d = nc.dram_tensor("bv", [1, H], F32, kind="ExternalInput").ap()
    am_d = nc.dram_tensor("amat", [H, H], F32, kind="ExternalInput").ap()
    clipx_d = nc.dram_tensor("clipx", [1, 1], F32, kind="ExternalInput").ap()
    clipd_d = nc.dram_tensor("clipd", [1, 1], F32, kind="ExternalInput").ap()
    wg_d = nc.dram_tensor("wg", [II, H], F32, kind="ExternalInput").ap()
    wu_d = nc.dram_tensor("wu", [II, H], F32, kind="ExternalInput").ap()
    wd_d = nc.dram_tensor("wd", [H, II], F32, kind="ExternalInput").ap()
    out_d = nc.dram_tensor("out", [S, H], F32, kind="ExternalOutput").ap()

    wgq = nc.dram_tensor("wgq", [II, H], BF16).ap()
    wuq = nc.dram_tensor("wuq", [II, H], BF16).ap()
    wdq = nc.dram_tensor("wdq", [H, II], BF16).ap()
    wqb = nc.dram_tensor("wqb", [H, H], BF16).ap()
    wvb = nc.dram_tensor("wvb", [H, H], BF16).ap()
    xqd = [nc.dram_tensor(f"xqd{i}", [S, H], BF16).ap() for i in range(2)]
    scd = nc.dram_tensor("scd", [S, H], BF16).ap()
    h_d = nc.dram_tensor("h_d", [II, S], BF16).ap()
    lmx_d = nc.dram_tensor("lmx_d", [4, P], F32).ap()
    cci = [nc.dram_tensor(f"cci{i}", [1, 1], F32).ap() for i in range(4)]
    cco = [nc.dram_tensor(f"cco{i}", [8, 1], F32).ap() for i in range(4)]

    sb = nc.alloc_sbuf_tensor
    x_res = sb("x_res", [P, NT * H], F32)
    act_fm = sb("act_fm", [P, HC * S], BF16)
    aq = sb("aq", [P, 16384], BF16)
    av = sb("av", [P, 16384], BF16)
    ae = sb("ae", [P, 16384], BF16)
    fscr = sb("fscr", [P, 2048], F32)
    i8s = sb("i8s", [P, 2048], I8)
    st = sb("stats", [P, 384], F32)
    bvr = sb("bvr", [1, H], BF16)
    onesr = sb("onesr", [1, P], BF16)
    onesc = sb("onesc", [P, 1], BF16)
    onesf = sb("onesf", [1, P], F32)
    bqf = sb("bqf", [P, HC], F32)
    sc0 = sb("sc0", [1, 16], F32)
    lmrow = sb("lmrow", [1, P], F32)
    garow = sb("garow", [1, 8], F32)

    pb = [nc.alloc_psum_tensor(f"pb{i}", [P, 512], F32) for i in range(8)]

    C_MVP, C_MEAN, C_VAR, C_RSTD, C_C1, C_C2 = 0, 32, 48, 64, 80, 96
    C_XMAX, C_XMIN, C_D1, C_D2, C_LM = 112, 128, 144, 160, 176
    C_INORM, C_GE, C_G, C_GB = 192, 208, 224, 240
    C_LC, C_HM, C_HT, C_PWS = 256, 257, 258, 284
    C_SCB, C_SHB, C_RS, C_CS, C_PW = 260, 264, 268, 272, 312

    S_CLX, S_CLD, S_SGX, S_SGD, S_T0, S_T1, S_ROW = 0, 1, 2, 3, 4, 5, 6

    def dma(q, out, in_, r=(), w=(), tag="", tp=False):
        if tp:
            return b.op(q, (lambda o_, i_: lambda e: e.dma_start(out=o_, in_=i_, transpose=True))(out, in_), r=r, w=w, dma=True, tag=tag)
        return b.op(q, (lambda o_, i_: lambda e: e.dma_start(out=o_, in_=i_))(out, in_), r=r, w=w, dma=True, tag=tag)

    def mm(out, lhsT, rhs, start, stop, r=(), w=(), tag=""):
        return b.op("tensor", (lambda o_, l_, rh_, s1, s2: lambda e: e.matmul(o_, l_, rh_, start=s1, stop=s2))(out, lhsT, rhs, start, stop), r=r, w=w, tag=tag)

    def act(out, in_, func, r=(), w=(), bias=0.0, scale=1.0, tag=""):
        return b.op("scalar", (lambda o_, i_, f_, b_, s_: lambda e: e.activation(out=o_, in_=i_, func=f_, bias=b_, scale=s_))(out, in_, func, bias, scale), r=r, w=w, tag=tag)

    def vcopy(out, in_, r=(), w=(), tag=""):
        return b.op("vector", (lambda o_, i_: lambda e: e.tensor_copy(out=o_, in_=i_))(out, in_), r=r, w=w, tag=tag)

    def vts(out, in0, s1, s2, op0, op1=None, r=(), w=(), tag=""):
        if op1 is None:
            return b.op("vector", (lambda o_, i_, a_, p0: lambda e: e.tensor_scalar(out=o_, in0=i_, scalar1=a_, scalar2=None, op0=p0))(out, in0, s1, op0), r=r, w=w, tag=tag)
        return b.op("vector", (lambda o_, i_, a_, b_, p0, p1: lambda e: e.tensor_scalar(out=o_, in0=i_, scalar1=a_, scalar2=b_, op0=p0, op1=p1))(out, in0, s1, s2, op0, op1), r=r, w=w, tag=tag)

    def vtt(out, in0, in1, op, r=(), w=(), tag=""):
        return b.op("vector", (lambda o_, i0, i1, p: lambda e: e.tensor_tensor(out=o_, in0=i0, in1=i1, op=p))(out, in0, in1, op), r=r, w=w, tag=tag)

    def vstt(out, in0, scalar, in1, op0, op1, r=(), w=(), tag=""):
        return b.op("vector", (lambda o_, i0, s_, i1, p0, p1: lambda e: e.scalar_tensor_tensor(out=o_, in0=i0, scalar=s_, in1=i1, op0=p0, op1=p1))(out, in0, scalar, in1, op0, op1), r=r, w=w, tag=tag)

    def vred(out, in_, op, absval=False, r=(), w=(), tag=""):
        return b.op("vector", (lambda o_, i_, p, a_: lambda e: e.tensor_reduce(out=o_, in_=i_, axis=AX.X, op=p, apply_absolute_value=a_))(out, in_, op, absval), r=r, w=w, tag=tag)

    def vrecip(out, in_, r=(), w=(), tag=""):
        return b.op("vector", (lambda o_, i_: lambda e: e.reciprocal(out=o_, in_=i_))(out, in_), r=r, w=w, tag=tag)

    def vmemset(ap_, val, w=(), tag=""):
        return b.op("vector", (lambda a_, v_: lambda e: e.memset(a_, v_))(ap_, val), w=w, tag=tag)

    def vbnstats(out, in_, r=(), w=(), tag=""):
        return b.op("vector", (lambda o_, i_: lambda e: e.bn_stats(out=o_, in_=i_))(out, in_), r=r, w=w, tag=tag)

    def vbnaggr(out, in_, r=(), w=(), tag=""):
        return b.op("vector", (lambda o_, i_: lambda e: e.bn_aggr(out=o_, in_=i_))(out, in_), r=r, w=w, tag=tag)

    def xt(t):
        return x_res[:, t * H:(t + 1) * H]

    for t in range(NT):
        dma("sync", xt(t), x_in[t * P:(t + 1) * P, :], w=[f"X{t}"], tag=f"ldx{t}")
    dma("sync", sc0[:, S_CLX:S_CLX + 1], clipx_d, w=["SC_clx"], tag="ldclx")
    dma("sync", sc0[:, S_CLD:S_CLD + 1], clipd_d, w=["SC_cld"], tag="ldcld")
    act(sc0[:, S_SGX:S_SGX + 1], sc0[:, S_CLX:S_CLX + 1], AF.Sigmoid,
        r=["SC_clx"], w=["SC_sgx"], tag="sigx")
    act(sc0[:, S_SGD:S_SGD + 1], sc0[:, S_CLD:S_CLD + 1], AF.Sigmoid,
        r=["SC_cld"], w=["SC_sgd"], tag="sigd")
    b.op("gpsimd", lambda e: e.dma_start(out=bvr[:], in_=bv_d), w=["BV"], dma=True, tag="ldbv")
    dma("sync", bqf[:], bq_d, w=["BQF"], tag="ldbq")
    vmemset(onesr[:], 1.0, w=["ONESR"], tag="onesr")
    vmemset(onesc[:], 1.0, w=["ONESC"], tag="onesc")
    vmemset(onesf[:], 1.0, w=["ONESF"], tag="onesf")
    for oc in range(HC):
        b.op("gpsimd",
             (lambda o_, i_: lambda e: e.dma_start(out=o_, in_=i_))(
                 ae[:, 8192 + oc * 1024:8192 + (oc + 1) * 1024],
                 am_d[oc * P:(oc + 1) * P, :]),
             w=_regs("AE", 8192 + oc * 1024, 1024, 4096), dma=True, tag=f"ldA{oc}")

    avf = av.ap().bitcast(F32)
    aei8 = ae.ap().bitcast(I8)

    def prep_gu_a(w_src, dkey, r_tile, idx):
        p4 = idx % 4
        fs = avf[:, 2048 + p4 * 1024:2048 + (p4 + 1) * 1024]
        fk = _regs("AV", 4096 + p4 * 2048, 2048, 512)
        pw = st[:, C_PW + 3 * p4:C_PW + 3 * p4 + 3]
        pk = [f"STpw{p4}"]
        dma("sync", fs, w_src[r_tile * P:(r_tile + 1) * P, :], w=fk, tag=f"pl{dkey}{r_tile}")
        vred(pw[:, 0:1], fs, ALU.max, absval=True, r=fk, w=pk)
        vrecip(pw[:, 1:2], pw[:, 0:1], r=pk, w=pk)
        vts(pw[:, 1:2], pw[:, 1:2], QMAX, None, ALU.mult, r=pk, w=pk)
        vts(pw[:, 2:3], pw[:, 0:1], 1.0 / QMAX, None, ALU.mult, r=pk, w=pk)
        act(fs, fs, AF.Identity, scale=pw[:, 1:2], r=fk + pk, w=fk)

    def prep_gu_b(w_dst, dkey, r_tile, idx):
        p4 = idx % 4
        fs = avf[:, 2048 + p4 * 1024:2048 + (p4 + 1) * 1024]
        fk = _regs("AV", 4096 + p4 * 2048, 2048, 512)
        pw = st[:, C_PW + 3 * p4:C_PW + 3 * p4 + 3]
        pk = [f"STpw{p4}"]
        i8 = aei8[:, p4 * 1024:(p4 + 1) * 1024]
        ik = ["AE0"]
        avs = av[:, p4 * 1024:(p4 + 1) * 1024]
        avk = _regs("AV", p4 * 1024, 1024, 512)
        vcopy(i8, fs, r=fk, w=ik)
        act(avs, i8, AF.Identity, scale=pw[:, 2:3], r=ik + pk, w=avk)
        dma("sync", w_dst[r_tile * P:(r_tile + 1) * P, :], avs, r=avk,
            w=[f"{dkey}{r_tile}"], tag=f"ps{dkey}{r_tile}")

    HF = II // 2

    def wd_pass1(r_tile):
        p4 = r_tile % 4
        fs = avf[:, 2048 + p4 * 1408:2048 + (p4 + 1) * 1408]
        fk = _regs("AV", 4096 + p4 * 2816, 2816, 512)
        pw = st[:, C_PW + 12 + 2 * (r_tile % 2):C_PW + 14 + 2 * (r_tile % 2)]
        pk = [f"STpwd{r_tile % 2}"]
        pws = st[:, C_PWS + r_tile:C_PWS + r_tile + 1]
        psk = ["STpws"]
        rows = wd_d[r_tile * P:(r_tile + 1) * P, :]
        dma("sync", fs, rows[:, 0:HF], w=fk, tag=f"plwd{r_tile}a")
        vred(pw[:, 0:1], fs, ALU.max, absval=True, r=fk, w=pk)
        dma("sync", fs, rows[:, HF:II], w=fk, tag=f"plwd{r_tile}b")
        vred(pw[:, 1:2], fs, ALU.max, absval=True, r=fk, w=pk)
        vtt(pws, pw[:, 0:1], pw[:, 1:2], ALU.max, r=pk, w=psk)

    def wd_pass2(r_tile, hf):
        p4 = (2 * r_tile + hf) % 4
        fs = avf[:, 6144 + (p4 % 2) * 640:6144 + (p4 % 2) * 640 + 1408]
        fk = _regs("AV", 12288 + (p4 % 2) * 1280, 2816, 512)
        pws = st[:, C_PWS + r_tile:C_PWS + r_tile + 1]
        pw2 = st[:, C_PW + 16 + p4:C_PW + 17 + p4]
        p2k = [f"STpwq{p4}"]
        psk = ["STpws"]
        i8off = (p4 % 2) * 640
        i8 = i8s[:, i8off:i8off + HF]
        ik = _regs("I8", i8off, HF, 512)
        avs = av[:, 8192 + (p4 % 2) * 1408:8192 + (p4 % 2 + 1) * 1408]
        avk = _regs("AV", 8192 + (p4 % 2) * 1408, 1408, 512)
        rows = wd_d[r_tile * P:(r_tile + 1) * P, :]
        dma("sync", fs, rows[:, hf * HF:(hf + 1) * HF], w=fk, tag=f"plwd{r_tile}c{hf}")
        vrecip(pw2, pws, r=psk, w=p2k)
        vts(pw2, pw2, QMAX, None, ALU.mult, r=p2k, w=p2k)
        act(fs, fs, AF.Identity, scale=pw2, r=fk + p2k, w=fk)
        vcopy(i8, fs, r=fk, w=ik)
        vts(pw2, pws, 1.0 / QMAX, None, ALU.mult, r=psk, w=p2k)
        act(avs, i8, AF.Identity, scale=pw2, r=ik + p2k, w=avk)
        dma("sync", wdq[r_tile * P:(r_tile + 1) * P, hf * HF:(hf + 1) * HF],
            avs, r=avk, w=[f"WDQ{r_tile}"], tag=f"pswd{r_tile}{hf}")

    def prep_attw(w_src, w_dst, dkey, r_tile):
        par = r_tile % 2
        avs = av[:, 14336 + par * 1024:14336 + (par + 1) * 1024]
        avk = _regs("AV", 14336 + par * 1024, 1024, 512)
        b.op("gpsimd",
             (lambda o_, i_: lambda e: e.dma_start(out=o_, in_=i_))(
                 avs, w_src[r_tile * P:(r_tile + 1) * P, :]),
             w=avk, dma=True, tag=f"pc{dkey}{r_tile}")
        dma("sync", w_dst[r_tile * P:(r_tile + 1) * P, :], avs, r=avk,
            w=[f"{dkey}{r_tile}"], tag=f"pw{dkey}{r_tile}")

    gu_specs = []
    for r_ in range(ICN):
        gu_specs.append((wg_d, wgq, "WGQ", r_))
        gu_specs.append((wu_d, wuq, "WUQ", r_))
    gu_state = {"a": 0, "b": 0}

    def gu_step(n):
        for _ in range(n):
            ia, ib_ = gu_state["a"], gu_state["b"]
            if ia < len(gu_specs):
                ws, _, dk, rr = gu_specs[ia]
                prep_gu_a(ws, dk, rr, ia)
                gu_state["a"] += 1
            if ia - ib_ >= 2 or (ia >= len(gu_specs) and ib_ < len(gu_specs)):
                _, wdst, dk, rr = gu_specs[ib_]
                prep_gu_b(wdst, dk, rr, ib_)
                gu_state["b"] += 1

    def gu_done():
        return gu_state["b"] >= len(gu_specs)

    prep_tasks = [(lambda rr=r_, hh=h_: wd_pass2(rr, hh))
                  for r_ in range(HC) for h_ in range(2)]
    attw_tasks = ([(lambda rr=r_: prep_attw(wq_d, wqb, "WQB", rr)) for r_ in range(HC)]
                  + [(lambda rr=r_: prep_attw(wv_d, wvb, "WVB", rr)) for r_ in range(HC)])

    def ln_stats_tile(t, want_minmax):
        mvp = st[:, C_MVP:C_MVP + 32]
        vbnstats(st[:, 300:306], xt(t)[:, 0:512], r=[f"X{t}"], w=["STb6"])
        vbnstats(st[:, 306:312], xt(t)[:, 512:1024], r=[f"X{t}"], w=["STb6"])
        vbnaggr(mvp[:, 2 * t:2 * t + 2], st[:, 300:312], r=["STb6"], w=["STmvp"])
        if want_minmax:
            vred(st[:, C_XMAX + t:C_XMAX + t + 1], xt(t), ALU.max,
                 r=[f"X{t}"], w=["STxmax"])
            vred(st[:, C_XMIN + t:C_XMIN + t + 1], xt(t), ALU.min,
                 r=[f"X{t}"], w=["STxmin"])

    def ln_stats_finish():
        mvp = st[:, C_MVP:C_MVP + 32]
        vcopy(st[:, C_MEAN:C_MEAN + 16], mvp[:, 0:32:2], r=["STmvp"], w=["STmean"])
        vcopy(st[:, C_VAR:C_VAR + 16], mvp[:, 1:32:2], r=["STmvp"], w=["STvar"])
        vts(st[:, C_RSTD:C_RSTD + 16], st[:, C_VAR:C_VAR + 16], EPS_LN, None,
            ALU.add, r=["STvar"], w=["STrstd"])
        act(st[:, C_RSTD:C_RSTD + 16], st[:, C_RSTD:C_RSTD + 16], AF.Sqrt,
            r=["STrstd"], w=["STrstd"])
        vrecip(st[:, C_RSTD:C_RSTD + 16], st[:, C_RSTD:C_RSTD + 16],
               r=["STrstd"], w=["STrstd"])

    def global_scale(k, lmax_src_key, sig_slot, sig_key, fold, cdst, ckey):
        dma("gpsimd", cci[k], sc0[:, S_T0:S_T0 + 1], r=[lmax_src_key],
            w=[f"CCI{k}"], tag=f"cci{k}")
        b.op("gpsimd",
             (lambda i_, o_: lambda e: e.collective_compute(
                 "AllGather", ALU.bypass, replica_groups=[list(range(8))],
                 ins=[i_], outs=[o_]))(cci[k], cco[k]),
             r=[f"CCI{k}"], w=[f"CCO{k}"], coll=True, tag=f"cc{k}")
        dma("gpsimd", garow[:], cco[k].rearrange("a b -> b a"), r=[f"CCO{k}"],
            w=["GAROW"], tag=f"ga{k}")
        vred(sc0[:, S_T1:S_T1 + 1], garow[:], ALU.max, r=["GAROW"], w=["SC_t1"])
        vtt(sc0[:, S_T1:S_T1 + 1], sc0[:, S_T1:S_T1 + 1], sc0[:, sig_slot:sig_slot + 1],
            ALU.mult, r=["SC_t1", sig_key], w=["SC_t1"])
        vts(sc0[:, S_T1:S_T1 + 1], sc0[:, S_T1:S_T1 + 1], 1.0 / QMAX, None,
            ALU.mult, r=["SC_t1"], w=["SC_t1"])
        vrecip(sc0[:, S_ROW + 1:S_ROW + 2], sc0[:, S_T1:S_T1 + 1],
               r=["SC_t1"], w=["SC_row"])
        vts(sc0[:, S_ROW:S_ROW + 1], sc0[:, S_T1:S_T1 + 1], fold, None,
            ALU.mult, r=["SC_t1"], w=["SC_row"])
        mm(pb[6][:, 0:2], onesf[0:1, :], sc0[0:1, S_ROW:S_ROW + 2], True, True,
           r=["ONESF", "SC_row"], w=["PB6"], tag=f"bc{k}")
        vcopy(st[:, cdst:cdst + 2], pb[6][:, 0:2], r=["PB6"], w=[ckey])

    def local_lnmax(k):
        c = st
        vtt(c[:, C_D1:C_D1 + 16], c[:, C_XMAX:C_XMAX + 16], c[:, C_MEAN:C_MEAN + 16],
            ALU.subtract, r=["STxmax", "STmean"], w=["STd1"])
        vtt(c[:, C_D2:C_D2 + 16], c[:, C_MEAN:C_MEAN + 16], c[:, C_XMIN:C_XMIN + 16],
            ALU.subtract, r=["STmean", "STxmin"], w=["STd2"])
        vtt(c[:, C_D1:C_D1 + 16], c[:, C_D1:C_D1 + 16], c[:, C_D2:C_D2 + 16],
            ALU.max, r=["STd1", "STd2"], w=["STd1"])
        vtt(c[:, C_LM:C_LM + 16], c[:, C_D1:C_D1 + 16], c[:, C_RSTD:C_RSTD + 16],
            ALU.mult, r=["STd1", "STrstd"], w=["STlm"])
        vred(c[:, C_LC:C_LC + 1], c[:, C_LM:C_LM + 16], ALU.max, r=["STlm"], w=["STlc"])
        dma("sync", lmx_d[k:k + 1, :].rearrange("a b -> b a"), c[:, C_LC:C_LC + 1],
            r=["STlc"], w=[f"LMX{k}"], tag=f"lmo{k}")
        dma("sync", lmrow[:], lmx_d[k:k + 1, :], r=[f"LMX{k}"], w=["LMROW"], tag=f"lmi{k}")
        vred(sc0[:, S_T0:S_T0 + 1], lmrow[:], ALU.max, r=["LMROW"], w=["SC_t0"])

    def ffn(call, stats_hooked=False):
        k_ln, k_h = 2 * call, 2 * call + 1
        xq_d = xqd[call]
        if not stats_hooked:
            for t in range(NT):
                ln_stats_tile(t, True)
        ln_stats_finish()
        local_lnmax(k_ln)
        global_scale(k_ln, "SC_t0", S_SGX, "SC_sgx", 1.0, C_SCB, "STscb")
        vts(st[:, C_C1:C_C1 + 16], st[:, C_RSTD:C_RSTD + 16],
            st[:, C_SCB + 1:C_SCB + 2], None, ALU.mult,
            r=["STrstd", "STscb"], w=["STc1"])
        vstt(st[:, C_C2:C_C2 + 16], st[:, C_MEAN:C_MEAN + 16], -1.0,
             st[:, C_C1:C_C1 + 16], ALU.mult, ALU.mult,
             r=["STmean", "STc1"], w=["STc2"])
        def xq_a(t):
            par = t % 2
            fs = fscr[:, par * 1024:(par + 1) * 1024]
            fk = _regs("FS", par * 1024, 1024, 512)
            act(fs, xt(t), AF.Identity, scale=st[:, C_C1 + t:C_C1 + t + 1],
                bias=st[:, C_C2 + t:C_C2 + t + 1],
                r=[f"X{t}", "STc1", "STc2"], w=fk, tag=f"xqa{t}")

        def xq_b(t):
            par = t % 2
            fs = fscr[:, par * 1024:(par + 1) * 1024]
            fk = _regs("FS", par * 1024, 1024, 512)
            i8 = i8s[:, par * 1024:(par + 1) * 1024]
            ik = _regs("I8", par * 1024, 1024, 512)
            vcopy(i8, fs, r=fk, w=ik)

        def xq_c(t):
            par = t % 2
            i8 = i8s[:, par * 1024:(par + 1) * 1024]
            ik = _regs("I8", par * 1024, 1024, 512)
            avs = av[:, 14336 + par * 1024:14336 + (par + 1) * 1024]
            avk = _regs("AV", 14336 + par * 1024, 1024, 512)
            act(avs, i8, AF.Identity, scale=st[:, C_SCB:C_SCB + 1],
                r=ik + ["STscb"], w=avk, tag=f"xqd{t}")
            dma("scalar", xq_d[t * P:(t + 1) * P, :], avs, r=avk,
                w=[f"XQ{call}_{t // 4}"], tag=f"xqs{t}")

        for t in range(NT + 2):
            if t < NT:
                xq_a(t)
            if 1 <= t <= NT:
                xq_b(t - 1)
            if t >= 2:
                xq_c(t - 2)
            if call == 0:
                gu_step(3)
        if call == 0:
            while not gu_done():
                gu_step(1)
            for r_ in range(HC):
                wd_pass1(r_)
        for hc in range(HC):
            for q4 in range(4):
                dma("sync", act_fm[:, hc * S + q4 * 512:hc * S + (q4 + 1) * 512],
                    xq_d[q4 * 512:(q4 + 1) * 512, hc * P:(hc + 1) * P],
                    r=[f"XQ{call}_{q4}"], w=[f"AF{hc}q{q4}"], tp=True,
                    tag=f"xqt{hc}_{q4}")
        vmemset(st[:, C_HM:C_HM + 1], 0.0, w=["SThm"], tag="hm0")
        ibs = [(0, 4), (4, 4), (8, 4), (12, 4), (16, 4), (20, 2)]
        it = 0
        for ibi, (ic0, nic) in enumerate(ibs):
            par = ibi % 2
            for wti, wsrc in enumerate((wgq, wuq)):
                base = (wti * 2 + par) * 4096
                for hc in range(HC):
                    dst = aq[:, base + hc * nic * P: base + (hc + 1) * nic * P]
                    dma("sync", dst, wsrc[ic0 * P:(ic0 + nic) * P, hc * P:(hc + 1) * P],
                        r=[f"{'WGQ' if wti == 0 else 'WUQ'}{ic0 + j}" for j in range(nic)],
                        w=_regs("AQ", base + hc * nic * P, nic * P, 2048), tp=True,
                        tag=f"g{call}w{ibi}_{wti}_{hc}")
            for icw in range(nic):
                ic = ic0 + icw
                for sbk in range(4):
                    gbank, ubank = it % 4, 4 + it % 4
                    for wti, bank in ((0, gbank), (1, ubank)):
                        base = (wti * 2 + par) * 4096
                        rk = _regs("AQ", base, 4096, 2048)
                        for hc in range(HC):
                            lhsT = aq[:, base + hc * nic * P + icw * P:
                                      base + hc * nic * P + (icw + 1) * P]
                            rhs = act_fm[:, hc * S + sbk * 512:hc * S + (sbk + 1) * 512]
                            mm(pb[bank][:, :], lhsT, rhs, hc == 0, hc == HC - 1,
                               r=rk + [f"AF{hc}q{sbk}"], w=[f"PB{bank}"],
                               tag=f"g{call}m{ic}_{sbk}_{wti}")
                    p4 = it % 4
                    sil = av[:, 4096 + p4 * 512:4096 + (p4 + 1) * 512]
                    silk = _regs("AV", 4096 + p4 * 512, 512, 512)
                    hst = av[:, 6144 + p4 * 512:6144 + (p4 + 1) * 512]
                    hk = _regs("AV", 6144 + p4 * 512, 512, 512)
                    it += 1
                    act(sil, pb[gbank][:, :], AF.Silu, r=[f"PB{gbank}"], w=silk,
                        tag=f"sil{call}_{ic}_{sbk}")
                    vtt(hst, sil, pb[ubank][:, :], ALU.mult,
                        r=silk + [f"PB{ubank}"], w=hk, tag=f"hmul{call}_{ic}_{sbk}")
                    vred(st[:, C_HT:C_HT + 1], hst, ALU.max, absval=True,
                         r=hk, w=["STht"])
                    vtt(st[:, C_HM:C_HM + 1], st[:, C_HM:C_HM + 1],
                        st[:, C_HT:C_HT + 1], ALU.max, r=["SThm", "STht"], w=["SThm"])
                    dma("gpsimd", h_d[ic * P:(ic + 1) * P, sbk * 512:(sbk + 1) * 512],
                        hst, r=hk, w=[f"HD{ic}_{sbk}"], tag=f"hs{call}_{ic}_{sbk}")
            if call == 0:
                for _ in range(3):
                    if prep_tasks:
                        prep_tasks.pop(0)()
        while call == 0 and prep_tasks:
            prep_tasks.pop(0)()
        vcopy(st[:, C_LC:C_LC + 1], st[:, C_HM:C_HM + 1], r=["SThm"], w=["STlc"])
        dma("sync", lmx_d[k_h:k_h + 1, :].rearrange("a b -> b a"),
            st[:, C_LC:C_LC + 1], r=["STlc"], w=[f"LMX{k_h}"], tag=f"lmo{k_h}")
        dma("sync", lmrow[:], lmx_d[k_h:k_h + 1, :], r=[f"LMX{k_h}"], w=["LMROW"],
            tag=f"lmi{k_h}")
        vred(sc0[:, S_T0:S_T0 + 1], lmrow[:], ALU.max, r=["LMROW"], w=["SC_t0"])
        global_scale(k_h, "SC_t0", S_SGD, "SC_sgd", 0.5, C_SHB, "STshb")
        def down_a(g, ic):
            p4 = ic % 4
            hin = av[:, 8192 + p4 * 512:8192 + (p4 + 1) * 512]
            hink = _regs("AV", 8192 + p4 * 512, 512, 512)
            dma("gpsimd", hin, h_d[ic * P:(ic + 1) * P, g * 512:(g + 1) * 512],
                r=[f"HD{ic}_{g}"], w=hink, tag=f"hl{call}_{g}_{ic}")
            wbase = 12288 + p4 * 1024
            wds = aq[:, wbase:wbase + 1024]
            wdk = _regs("AQ", wbase, 1024, 2048)
            dma("sync", wds, wdq[0:H, ic * P:(ic + 1) * P],
                r=[f"WDQ{rr}" for rr in range(HC)], w=wdk, tp=True,
                tag=f"wdl{call}_{g}_{ic}")
            fs = fscr[:, p4 * 512:(p4 + 1) * 512]
            fk = [f"FS{p4}"]
            act(fs, hin, AF.Identity, scale=st[:, C_SHB + 1:C_SHB + 2],
                r=hink + ["STshb"], w=fk, tag=f"hqa{call}_{g}_{ic}")
            i8 = i8s[:, p4 * 512:(p4 + 1) * 512]
            ik = _regs("I8", p4 * 512, 512, 512)
            vcopy(i8, fs, r=fk, w=ik)

        def down_b(g, ic):
            p4 = ic % 4
            i8 = i8s[:, p4 * 512:(p4 + 1) * 512]
            ik = _regs("I8", p4 * 512, 512, 512)
            hq = av[:, 10240 + p4 * 512:10240 + (p4 + 1) * 512]
            hqk = _regs("AV", 10240 + p4 * 512, 512, 512)
            act(hq, i8, AF.Identity, scale=st[:, C_SHB:C_SHB + 1],
                r=ik + ["STshb"], w=hqk, tag=f"hqd{call}_{g}_{ic}")
            wbase = 12288 + p4 * 1024
            wds = aq[:, wbase:wbase + 1024]
            wdk = _regs("AQ", wbase, 1024, 2048)
            for st4 in range(4):
                for hb in range(2):
                    bank = st4 * 2 + hb
                    mm(pb[bank][:, :], hq[:, st4 * P:(st4 + 1) * P],
                       wds[:, hb * 512:(hb + 1) * 512], ic == 0, ic == ICN - 1,
                       r=hqk + wdk, w=[f"PB{bank}"], tag=f"dm{call}_{g}_{ic}")

        for g in range(4):
            down_a(g, 0)
            for ic in range(1, ICN):
                down_a(g, ic)
                down_b(g, ic - 1)
            down_b(g, ICN - 1)
            for st4 in range(4):
                t = g * 4 + st4
                for hb in range(2):
                    bank = st4 * 2 + hb
                    vtt(xt(t)[:, hb * 512:(hb + 1) * 512], pb[bank][:, :],
                        xt(t)[:, hb * 512:(hb + 1) * 512], ALU.add,
                        r=[f"PB{bank}", f"X{t}"], w=[f"X{t}"], tag=f"dres{call}_{t}_{hb}")
            if call == 0:
                for st4 in range(4):
                    ln_stats_tile(g * 4 + st4, False)
                for _ in range(4):
                    if attw_tasks:
                        attw_tasks.pop(0)()
            elif post_down_hook is not None:
                post_down_hook(call, g)

    post_down_hook = None

    def attn(post_sb_hook=None):
        ln_stats_finish()
        c = st
        vts(c[:, C_INORM:C_INORM + 16], c[:, C_VAR:C_VAR + 16], float(H), None,
            ALU.mult, r=["STvar"], w=["STin"])
        act(c[:, C_INORM:C_INORM + 16], c[:, C_INORM:C_INORM + 16], AF.Sqrt,
            r=["STin"], w=["STin"])
        vtt(c[:, C_INORM:C_INORM + 16], c[:, C_INORM:C_INORM + 16],
            c[:, C_RSTD:C_RSTD + 16], ALU.mult, r=["STin", "STrstd"], w=["STin"])
        vts(c[:, C_GE:C_GE + 16], c[:, C_INORM:C_INORM + 16], EPS_NRM, None,
            ALU.add, r=["STin"], w=["STge"])
        vrecip(c[:, C_GE:C_GE + 16], c[:, C_GE:C_GE + 16], r=["STge"], w=["STge"])
        vtt(c[:, C_G:C_G + 16], c[:, C_GE:C_GE + 16], c[:, C_RSTD:C_RSTD + 16],
            ALU.mult, r=["STge", "STrstd"], w=["STg"])
        vstt(c[:, C_GB:C_GB + 16], c[:, C_MEAN:C_MEAN + 16], -1.0,
             c[:, C_G:C_G + 16], ALU.mult, ALU.mult,
             r=["STmean", "STg"], w=["STgb"])
        for t in range(NT):
            par = t % 2
            avs = av[:, 14336 + par * 1024:14336 + (par + 1) * 1024]
            avk = _regs("AV", 14336 + par * 1024, 1024, 512)
            act(avs, xt(t), AF.Identity, scale=c[:, C_G + t:C_G + t + 1],
                bias=c[:, C_GB + t:C_GB + t + 1],
                r=[f"X{t}", "STg", "STgb"], w=avk, tag=f"sca{t}")
            dma("scalar", scd[t * P:(t + 1) * P, :], avs, r=avk,
                w=[f"SCD{t // 4}"], tag=f"scs{t}")
        for hc in range(HC):
            for q4 in range(4):
                dma("sync", act_fm[:, hc * S + q4 * 512:hc * S + (q4 + 1) * 512],
                    scd[q4 * 512:(q4 + 1) * 512, hc * P:(hc + 1) * P],
                    r=[f"SCD{q4}"], w=[f"AF{hc}q{q4}"], tp=True, tag=f"sct{hc}_{q4}")
        for hc in range(HC):
            dma("sync", ae[:, hc * 1024:(hc + 1) * 1024],
                wqb[0:H, hc * P:(hc + 1) * P], r=[f"WQB{rr}" for rr in range(HC)],
                w=_regs("AE", hc * 1024, 1024, 4096), tp=True, tag=f"wqT{hc}")
        for sbk in range(4):
            for oc in range(HC):
                bank = oc
                for hc in range(HC):
                    mm(pb[bank][:, :], ae[:, hc * 1024 + oc * P:hc * 1024 + (oc + 1) * P],
                       act_fm[:, hc * S + sbk * 512:hc * S + (sbk + 1) * 512],
                       hc == 0, hc == HC - 1,
                       r=_regs("AE", hc * 1024, 1024, 4096) + [f"AF{hc}q{sbk}"],
                       w=[f"PB{bank}"], tag=f"qm{oc}_{sbk}")
                act(aq[:, oc * S + sbk * 512:oc * S + (sbk + 1) * 512], pb[bank][:, :],
                    AF.Identity, bias=bqf[:, oc:oc + 1],
                    r=[f"PB{bank}", "BQF"],
                    w=_regs("AQ", oc * S + sbk * 512, 512, 2048), tag=f"qe{oc}_{sbk}")
        for hc in range(HC):
            dma("sync", ae[:, hc * 1024:(hc + 1) * 1024],
                wvb[0:H, hc * P:(hc + 1) * P], r=[f"WVB{rr}" for rr in range(HC)],
                w=_regs("AE", hc * 1024, 1024, 4096), tp=True, tag=f"wvT{hc}")
        for tt_ in range(NT):
            for ob in range(2):
                bank = (2 * tt_ + ob) % 8
                for hc in range(HC):
                    mm(pb[bank][:, :],
                       act_fm[:, hc * S + tt_ * P:hc * S + (tt_ + 1) * P],
                       ae[:, hc * 1024 + ob * 512:hc * 1024 + (ob + 1) * 512],
                       hc == 0, False,
                       r=_regs("AE", hc * 1024, 1024, 4096) + [f"AF{hc}q{tt_ // 4}"],
                       w=[f"PB{bank}"], tag=f"vm{tt_}_{ob}")
                mm(pb[bank][:, :], onesr[0:1, :], bvr[0:1, ob * 512:(ob + 1) * 512],
                   False, True, r=["ONESR", "BV"], w=[f"PB{bank}"], tag=f"vb{tt_}_{ob}")
                vcopy(av[:, tt_ * 1024 + ob * 512:tt_ * 1024 + (ob + 1) * 512],
                      pb[bank][:, :], r=[f"PB{bank}"],
                      w=_regs("AV", tt_ * 1024 + ob * 512, 512, 512),
                      tag=f"ve{tt_}_{ob}")
        for sbk in range(4):
            for kc in range(HC):
                bank = kc
                for oc in range(HC):
                    lhsT = ae[:, 8192 + oc * 1024 + kc * P:8192 + oc * 1024 + (kc + 1) * P]
                    rhs = aq[:, oc * S + sbk * 512:oc * S + (sbk + 1) * 512]
                    mm(pb[bank][:, :], lhsT, rhs, oc == 0, oc == HC - 1,
                       r=_regs("AE", 8192 + oc * 1024, 1024, 4096)
                       + _regs("AQ", oc * S + sbk * 512, 512, 2048),
                       w=[f"PB{bank}"], tag=f"tqm{kc}_{sbk}")
                act(act_fm[:, kc * S + sbk * 512:kc * S + (sbk + 1) * 512],
                    pb[bank][:, :], AF.Identity, r=[f"PB{bank}"],
                    w=[f"AF{kc}q{sbk}"], tag=f"tqe{kc}_{sbk}")
        for sbk in range(4):
            ebase = (sbk % 2) * 8192
            ek = lambda tc, eb=ebase: _regs("AE", eb + tc * 512, 512, 4096)
            for tc in range(NT):
                bank = tc % 3
                for oc in range(HC):
                    mm(pb[bank][:, :],
                       act_fm[:, oc * S + tc * P:oc * S + (tc + 1) * P],
                       aq[:, oc * S + sbk * 512:oc * S + (sbk + 1) * 512],
                       oc == 0, oc == HC - 1,
                       r=[f"AF{oc}q{tc // 4}"]
                       + _regs("AQ", oc * S + sbk * 512, 512, 2048),
                       w=[f"PB{bank}"], tag=f"sm{sbk}_{tc}")
                act(ae[:, ebase + tc * 512:ebase + (tc + 1) * 512], pb[bank][:, :],
                    AF.Exp, scale=1.0 / 32.0, r=[f"PB{bank}"], w=ek(tc),
                    tag=f"ex{sbk}_{tc}")
            for st4 in range(4):
                for tc in range(NT):
                    mm(pb[7][:, st4:st4 + 1],
                       ae[:, ebase + tc * 512 + st4 * P:ebase + tc * 512 + (st4 + 1) * P],
                       onesc[:, :], tc == 0, tc == NT - 1,
                       r=ek(tc) + ["ONESC"], w=["PB7"], tag=f"rs{sbk}_{st4}")
            vrecip(st[:, C_RS:C_RS + 4], pb[7][:, 0:4], r=["PB7"], w=["STrs"])
            vtt(st[:, C_CS:C_CS + 4], st[:, C_RS:C_RS + 4],
                st[:, C_INORM + sbk * 4:C_INORM + sbk * 4 + 4], ALU.mult,
                r=["STrs", "STin"], w=["STcs"])
            for ob in range(2):
                for st4 in range(4):
                    bank = 3 + st4
                    for tc in range(NT):
                        mm(pb[bank][:, :],
                           ae[:, ebase + tc * 512 + st4 * P:ebase + tc * 512 + (st4 + 1) * P],
                           av[:, tc * 1024 + ob * 512:tc * 1024 + (ob + 1) * 512],
                           tc == 0, tc == NT - 1,
                           r=ek(tc) + _regs("AV", tc * 1024 + ob * 512, 512, 512),
                           w=[f"PB{bank}"], tag=f"cm{sbk}_{ob}_{st4}")
                for st4 in range(4):
                    bank = 3 + st4
                    t = sbk * 4 + st4
                    vstt(xt(t)[:, ob * 512:(ob + 1) * 512], pb[bank][:, :],
                         st[:, C_CS + st4:C_CS + st4 + 1],
                         xt(t)[:, ob * 512:(ob + 1) * 512], ALU.mult, ALU.add,
                         r=[f"PB{bank}", f"X{t}", "STcs"], w=[f"X{t}"],
                         tag=f"ce{sbk}_{ob}_{st4}")
            if post_sb_hook is not None:
                post_sb_hook(sbk)

    if RUN_FFN1:
        ffn(0)
    while not gu_done():
        gu_step(1)
    while prep_tasks:
        prep_tasks.pop(0)()
    while attw_tasks:
        attw_tasks.pop(0)()
    if RUN_ATTN:
        def _hook(sbk):
            if RUN_FFN3:
                for st4 in range(4):
                    ln_stats_tile(sbk * 4 + st4, True)
        attn(post_sb_hook=_hook)
    if RUN_FFN3:
        ffn(1, stats_hooked=RUN_ATTN)

    for t in range(NT):
        dma("gpsimd", out_d[t * P:(t + 1) * P, :], xt(t), r=[f"X{t}"],
            w=[f"OUT{t}"], tag=f"sto{t}")

    counts = b.finalize(nc)
    return nc, counts


_CACHE = {}


def kernel(L0_attn_hidden_states, L0_attn_Wq, L0_attn_bq, L0_attn_Wv, L0_attn_bv,
           L0_attn_anomaly_matrix, L0_ffn_clip_x, L0_ffn_clip_down,
           L0_ffn_w_gate, L0_ffn_w_up, L0_ffn_w_down):
    hs = np.asarray(L0_attn_hidden_states, dtype=np.float32)
    nb = hs.shape[0]
    common = {
        "wq": np.ascontiguousarray(np.asarray(L0_attn_Wq, np.float32)),
        "wv": np.ascontiguousarray(np.asarray(L0_attn_Wv, np.float32)),
        "bq": np.ascontiguousarray(np.asarray(L0_attn_bq, np.float32).reshape(HC, P).T),
        "bv": np.ascontiguousarray(np.asarray(L0_attn_bv, np.float32).reshape(1, H)),
        "amat": np.ascontiguousarray(np.asarray(L0_attn_anomaly_matrix, np.float32)),
        "clipx": np.ascontiguousarray(np.asarray(L0_ffn_clip_x, np.float32).reshape(1, 1)),
        "clipd": np.ascontiguousarray(np.asarray(L0_ffn_clip_down, np.float32).reshape(1, 1)),
        "wg": np.ascontiguousarray(np.asarray(L0_ffn_w_gate, np.float32)),
        "wu": np.ascontiguousarray(np.asarray(L0_ffn_w_up, np.float32)),
        "wd": np.ascontiguousarray(np.asarray(L0_ffn_w_down, np.float32)),
    }
    if "nc" not in _CACHE:
        _CACHE["nc"] = build_program()
    nc, _ = _CACHE["nc"]
    in_maps = [dict(common, x_in=np.ascontiguousarray(hs[b_])) for b_ in range(nb)]
    res = run_bass_kernel_spmd(nc, in_maps, core_ids=list(range(nb)))
    out = np.stack([np.asarray(res.results[b_]["out"]) for b_ in range(nb)], axis=0)
    return out.astype(np.float32)


if __name__ == "__main__":
    nc, counts = build_program()
    print("per-engine sem counts:", counts)
    print("n_instructions:", nc.n_instructions())
    print("sbuf_bytes_remaining:", nc.sbuf_bytes_remaining)
